# version 02
# baseline (speedup 1.0000x reference)
import numpy as np
import concourse.bass as bass
import concourse.mybir as mybir
from concourse.bass_utils import run_bass_kernel_spmd

F32 = mybir.dt.float32
BF16 = mybir.dt.bfloat16
AF = mybir.ActivationFunctionType
ALU = mybir.AluOpType

P = 128
C = 512
DB = 8
HB = 32
D = 1024
H = 4096
NCH = 9
TOK = 4096
PINIT = -60.0
EPS_LN = 1e-5
EPS_WKV = 1e-9
NSLOT = 3
SLOT_COLS = 4096


class Plan:
    def __init__(self):
        self.ev = {}

    def rec(self, name, sem, val):
        assert name not in self.ev, f"dup event {name}"
        self.ev[name] = (sem, val)


class Ctx:
    SEM_STEP = {"x": 16, "o": 16, "ws0": 16, "ws1": 16, "ws2": 16,
                "cw0": 16, "cw1": 16,
                "t": 1, "v": 1, "s": 1, "g": 1}

    def __init__(self, plan, emit, handles=None):
        self.plan = plan
        self.emit = emit
        self.h = handles or {}
        self.cnt = {k: 0 for k in self.SEM_STEP}

    def alias(self, name, sem):
        if not self.emit:
            self.plan.rec(name, sem, self.cnt[sem])

    def inc(self, inst, sem, event=None, n=1):
        self.cnt[sem] += self.SEM_STEP[sem] * n
        if not self.emit:
            if event is not None:
                self.plan.rec(event, sem, self.cnt[sem])
        else:
            assert inst is not None, f"inc without instruction ({sem},{event})"
            inst.then_inc(self.h[sem], self.SEM_STEP[sem] * n)
        return self.cnt[sem]

    def wait(self, eng, event):
        if not self.emit:
            return
        sem, val = self.plan.ev[event]
        eng.wait_ge(self.h[sem], val)

    def wait_maybe(self, eng, event):
        if event is None or not self.emit:
            return
        sem, val = self.plan.ev[event]
        eng.wait_ge(self.h[sem], val)


class Mock:
    def __getattr__(self, name):
        return lambda *a, **k: _MOCK_INST


class _MockInst:
    def then_inc(self, *a, **k):
        return self


_MOCK_INST = _MockInst()



_SIG = {
    "tensor_copy": ([0], [1]),
    "tensor_tensor": ([0], [1, 2]),
    "tensor_add": ([0], [1, 2]),
    "tensor_sub": ([0], [1, 2]),
    "tensor_mul": ([0], [1, 2]),
    "tensor_max": ([0], [1, 2]),
    "tensor_scalar": ([0], [1, 2, 3]),
    "tensor_scalar_add": ([0], [1, 2]),
    "tensor_scalar_mul": ([0], [1, 2]),
    "scalar_tensor_tensor": ([0], [1, 2, 3]),
    "tensor_tensor_scan": ([0], [1, 2, 3]),
    "memset": ([0], []),
    "activation": ([0], [1]),
}


def _ap_range(a):
    if a is None or isinstance(a, (int, float)):
        return None
    try:
        t = a.tensor.name
        esz = mybir.dt.size(a.dtype)
        dims = list(a.ap)
    except AttributeError:
        return None
    span = 1
    for st, ct in dims[1:]:
        span += abs(st) * (ct - 1)
    lo = a.offset * esz
    return (t, lo, lo + span * esz)


class Guard:

    def __init__(self, eng):
        self._e = eng
        self._w = []
        self._r = []

    def __getattr__(self, name):
        e = self._e
        if name in _SIG:
            oi, ii = _SIG[name]

            def call(*args, **kwargs):
                outs = [_ap_range(args[i]) for i in oi if i < len(args)]
                ins = [_ap_range(args[i]) for i in ii if i < len(args)]
                for kw in ("bias", "scale"):
                    if kw in kwargs:
                        ins.append(_ap_range(kwargs[kw]))
                outs = [x for x in outs if x]
                ins = [x for x in ins if x]

                def hits(rg, lst):
                    return any(rg[0] == o[0] and rg[1] < o[2] and o[1] < rg[2]
                               for o in lst)
                need = any(hits(rg, self._w) for rg in ins) or                     any(hits(rg, self._w) or hits(rg, self._r) for rg in outs)
                if need:
                    e.drain()
                    self._w, self._r = [], []
                self._w.extend(outs)
                self._r.extend(ins)
                return getattr(e, name)(*args, **kwargs)
            return call
        if name == "drain":
            def call(*a, **k):
                self._w, self._r = [], []
                return e.drain(*a, **k)
            return call
        return getattr(e, name)


def build_program(nch=NCH):
    nc = bass.Bass("TRN2", target_bir_lowering=False)

    xT = nc.dram_tensor("xT", [D, nch * C], F32, kind="ExternalInput").ap()
    outT = nc.dram_tensor("outT", [D, TOK], F32, kind="ExternalOutput").ap()
    wkt = nc.dram_tensor("wkt", [DB, P, D], F32, kind="ExternalInput").ap()
    wvt = nc.dram_tensor("wvt", [DB, P, D], F32, kind="ExternalInput").ap()
    wrt = nc.dram_tensor("wrt", [DB, P, D], F32, kind="ExternalInput").ap()
    wot = nc.dram_tensor("wot", [DB, P, D], F32, kind="ExternalInput").ap()
    w1t = nc.dram_tensor("w1t", [HB, P, D], F32, kind="ExternalInput").ap()
    w2t = nc.dram_tensor("w2t", [HB, P, D], F32, kind="ExternalInput").ap()
    h1t = nc.dram_tensor("h1t", [HB, P, H], F32, kind="ExternalInput").ap()
    h2t = nc.dram_tensor("h2t", [HB, P, H], F32, kind="ExternalInput").ap()
    w3t = nc.dram_tensor("w3t", [DB, P, H], F32, kind="ExternalInput").ap()
    cattn = nc.dram_tensor("cattn", [P, 40], F32, kind="ExternalInput").ap()
    cffn = nc.dram_tensor("cffn", [P, 136], F32, kind="ExternalInput").ap()
    cwb = nc.dram_tensor("cwb", [P, 1], F32, kind="ExternalInput").ap()

    bat = {m: nc.dram_tensor(f"b_{m}", [DB, P, 2 * D], BF16, kind="Internal").ap()
           for m in ("k", "v", "r", "o")}
    b12 = nc.dram_tensor("b_12", [HB, P, 2 * D], BF16, kind="Internal").ap()
    bh1 = nc.dram_tensor("b_h1", [HB, P, H], BF16, kind="Internal").ap()
    bh2 = nc.dram_tensor("b_h2", [HB, P, H], BF16, kind="Internal").ap()
    b3 = nc.dram_tensor("b_3", [DB, P, H], BF16, kind="Internal").ap()

    ACOLS = 52600
    arena = nc.alloc_sbuf_tensor("arena", [P, ACOLS], F32).ap()
    off = [0]

    def calloc(cols_f32):
        o = off[0]
        off[0] += cols_f32
        assert off[0] <= ACOLS, f"arena overflow {off[0]}"
        return o

    def f32v(o, cols):
        return arena[:, o:o + cols]

    def bf16v(o, cols_bf16):
        assert cols_bf16 % 2 == 0
        return arena[:, o:o + cols_bf16 // 2].bitcast(BF16)

    o_x = calloc(DB * C)
    o_mix = calloc(DB * C)
    o_scr = calloc(4096)
    o_wbuf = calloc(NSLOT * SLOT_COLS // 2)
    o_cat = calloc(40)
    o_cff = calloc(136)
    o_cwb = calloc(1)
    o_negu = calloc(8)
    o_uwb = calloc(8)
    o_epsln = calloc(1)
    o_epsw = calloc(1)
    o_ones = calloc(128)
    o_pc = calloc(8)
    o_qc = calloc(8)
    o_lc = calloc(8)
    o_AB = calloc(32768)
    assert off[0] * 4 <= 210000, f"SBUF over budget: {off[0] * 4}"

    x_pl = [f32v(o_x + d * C, C) for d in range(DB)]
    mix_hi = [bf16v(o_mix + d * C // 2, C) for d in range(DB)]
    mix_lo = [bf16v(o_mix + (DB * C + d * C) // 2, C) for d in range(DB)]
    SCR = f32v(o_scr, 4096)
    wslot = [bf16v(o_wbuf + i * SLOT_COLS // 2, SLOT_COLS) for i in range(NSLOT)]
    cat = f32v(o_cat, 40)
    cff = f32v(o_cff, 136)
    tmk = [cat[:, d:d + 1] for d in range(0, 8)]
    tmv = [cat[:, d:d + 1] for d in range(8, 16)]
    tmr = [cat[:, d:d + 1] for d in range(16, 24)]
    wdec = [cat[:, d:d + 1] for d in range(24, 32)]
    ucol = [cat[:, d:d + 1] for d in range(32, 40)]
    s1c = [cff[:, e:e + 1] for e in range(0, 32)]
    s2c = [cff[:, e:e + 1] for e in range(32, 64)]
    h1sc = [cff[:, e:e + 1] for e in range(64, 96)]
    h2sc = [cff[:, e:e + 1] for e in range(96, 128)]
    s3c = [cff[:, e:e + 1] for e in range(128, 136)]
    wbcol = f32v(o_cwb, 1)
    negu = f32v(o_negu, 8)
    uwb = f32v(o_uwb, 8)
    epsln = f32v(o_epsln, 1)
    epsw = f32v(o_epsw, 1)
    ones = f32v(o_ones, 128)
    p_carry = f32v(o_pc, 8)
    q_carry = f32v(o_qc, 8)
    ln_carry = f32v(o_lc, 8)

    a_ln = o_AB
    a_P = a_ln + DB * 513
    a_Q = a_P + DB * 513
    a_v = a_Q + DB * 513
    a_uk = a_v + DB * C
    a_r = a_uk + DB * C
    assert a_r + DB * C - o_AB <= 32768
    ln_pl = [f32v(a_ln + d * 513, 513) for d in range(DB)]
    P_pl = [f32v(a_P + d * 513, 513) for d in range(DB)]
    Q_pl = [f32v(a_Q + d * 513, 513) for d in range(DB)]
    v_pl = [f32v(a_v + d * C, C) for d in range(DB)]
    uk_pl = [f32v(a_uk + d * C, C) for d in range(DB)]
    r_pl = [f32v(a_r + d * C, C) for d in range(DB)]
    ln_all = f32v(a_ln, DB * 513)
    P_all = f32v(a_P, DB * 513)
    Q_all = f32v(a_Q, DB * 513)
    mixscr = [f32v(a_P + d * 513, C) for d in range(DB)]

    h0_hi = [bf16v(o_AB + e * C // 2, C) for e in range(HB)]
    h0_lo = [bf16v(o_AB + (HB * C + e * C) // 2, C) for e in range(HB)]
    g1_hi = [bf16v(o_AB + (2 * HB * C + e * C) // 2, C) for e in range(HB)]
    g1_lo = [bf16v(o_AB + (3 * HB * C + e * C) // 2, C) for e in range(HB)]

    SA = [SCR[:, i * 1024:(i + 1) * 1024] for i in range(2)]
    SM = [SCR[:, 2048 + i * 512:2048 + (i + 1) * 512] for i in range(2)]
    SC_ = [SCR[:, 3072 + i * 512:3072 + (i + 1) * 512] for i in range(2)]
    sq_pl = [SCR[:, d * 512:(d + 1) * 512] for d in range(DB)]
    var_s = SCR[:, 0:512]
    rstd_s = SCR[:, 512:1024]
    dd_pl = [SCR[:, d * 512:(d + 1) * 512] for d in range(DB)]
    f_silu = [SCR[:, i * 512:(i + 1) * 512] for i in range(2)]
    f_h = [SCR[:, 1024 + i * 512:1024 + (i + 1) * 512] for i in range(2)]
    f_g = [SCR[:, 2048 + i * 512:2048 + (i + 1) * 512] for i in range(2)]
    conv_hi = [SCR[:, 0:1024].bitcast(BF16), SCR[:, 1024:2048].bitcast(BF16)]
    conv_lo = [SCR[:, 2048:3072].bitcast(BF16), SCR[:, 3072:4096].bitcast(BF16)]

    banks = [nc.alloc_psum_tensor(f"pb{i}", [P, 512], F32).ap() for i in range(8)]

    handles = {}
    guards = []
    for s in Ctx.SEM_STEP:
        g = nc.semaphore(f"sem_{s}")
        handles[s] = g.__enter__()
        guards.append(g)

    plan = Plan()

    loads = []
    for k in range(nch):
        mats = ["k", "v"] if k == 0 else ["k", "v", "r", "o"]
        for m in mats:
            for e in range(DB):
                loads.append(("at", m, e, k))
        if k > 0:
            for e in range(HB):
                loads.append(("w12", e, k))
            for e in range(HB):
                loads.append(("h1", e, k))
            for e in range(HB):
                loads.append(("h2", e, k))
            for e in range(DB):
                loads.append(("w3", e, k))
    load_idx = {ld: i for i, ld in enumerate(loads)}

    def load_src(ld):
        if ld[0] == "at":
            return bat[ld[1]][ld[2]], 2 * D
        if ld[0] == "w12":
            return b12[ld[1]], 2 * D
        if ld[0] == "h1":
            return bh1[ld[1]], H
        if ld[0] == "h2":
            return bh2[ld[1]], H
        return b3[ld[1]], H

    conv = []
    for m, src in (("k", wkt), ("v", wvt), ("r", wrt), ("o", wot)):
        for e in range(DB):
            conv.append((bat[m][e][:, 0:D], bat[m][e][:, D:2 * D], src[e], D))
    for e in range(HB):
        conv.append((b12[e][:, 0:D], None, w1t[e], D))
        conv.append((b12[e][:, D:2 * D], None, w2t[e], D))
    for e in range(HB):
        conv.append((bh1[e][:, 0:2048], None, h1t[e][:, 0:2048], 2048))
        conv.append((bh1[e][:, 2048:4096], None, h1t[e][:, 2048:4096], 2048))
    for e in range(HB):
        conv.append((bh2[e][:, 0:2048], None, h2t[e][:, 0:2048], 2048))
        conv.append((bh2[e][:, 2048:4096], None, h2t[e][:, 2048:4096], 2048))
    for e in range(DB):
        conv.append((b3[e][:, 0:2048], None, w3t[e][:, 0:2048], 2048))
        conv.append((b3[e][:, 2048:4096], None, w3t[e][:, 2048:4096], 2048))
    NCONV = len(conv)

    def emit_sync(E, ctx):
        ctx.inc(E.dma_start(out=cat, in_=cattn), "x")
        ctx.inc(E.dma_start(out=cff, in_=cffn), "x")
        ctx.inc(E.dma_start(out=wbcol, in_=cwb), "x", "c_consts")
        for i, (dsthi, dstlo, src, cols) in enumerate(conv):
            if i >= NSLOT:
                ctx.wait(E, f"conv_{i - NSLOT}")
            st = wslot[i % NSLOT].bitcast(F32)[:, 0:cols]
            ctx.inc(E.dma_start(out=st, in_=src), f"ws{i % NSLOT}",
                    f"convin_{i}")
            ctx.wait(E, f"conv_{i}")
            pi = i % 2
            ctx.inc(E.dma_start(out=dsthi, in_=conv_hi[pi][:, 0:cols]),
                    f"cw{i % 2}",
                    None if dstlo is not None else f"convw_{i}")
            if dstlo is not None:
                ctx.inc(E.dma_start(out=dstlo, in_=conv_lo[pi][:, 0:cols]),
                        f"cw{i % 2}", f"convw_{i}")
        li = 0
        for k in range(nch):
            if k == 1:
                ctx.wait(E, "ln1_0")
                ctx.wait(E, "sq1_0")
            elif k >= 2:
                ctx.wait(E, f"out_{k - 1}")
            src = xT[:, k * C:(k + 1) * C].rearrange("(d p) t -> p d t", p=P)
            dst = f32v(o_x, DB * C).rearrange("p (d t) -> p d t", d=DB)
            ctx.inc(E.dma_start(out=dst, in_=src), "x", f"xin_{k}")
            while li < len(loads) and loads[li][-1] == k:
                ld = loads[li]
                if li >= NSLOT:
                    ctx.wait(E, f"wfree_{li - NSLOT}")
                elif li == 0:
                    ctx.wait(E, f"convw_{NCONV - 1}")
                    ctx.wait(E, f"convw_{NCONV - 2}")
                src_ap, cols = load_src(ld)
                dst_ap = wslot[li % NSLOT][:, 0:cols]
                ctx.inc(E.dma_start(out=dst_ap, in_=src_ap), f"ws{li % NSLOT}",
                        f"wl_{li}")
                li += 1
            if k > 0:
                ctx.wait(E, f"w3resid_{k}_{DB - 1}")
                src_o = f32v(o_x, DB * C).rearrange("p (d t) -> p d t", d=DB)
                dst_o = outT[:, (k - 1) * C:k * C].rearrange(
                    "(d p) t -> p d t", p=P)
                ctx.inc(E.dma_start(out=dst_o, in_=src_o), "o", f"out_{k}")
        assert li == len(loads)
        if ctx.emit:
            E.wait_ge(ctx.h["o"], ctx.cnt["o"])

    def emit_gpsimd(E, ctx):
        E.memset(ones, 1.0 / 1024.0)
        E.memset(epsln, EPS_LN)
        E.memset(epsw, EPS_WKV)
        E.memset(p_carry, PINIT)
        E.memset(q_carry, 0.0)
        ctx.inc(E.memset(ln_carry, 0.0), "g", "g_init")

    def _qscan(E, ctx, k, b):
        pi = b % 2
        ctx.wait(E, f"sexp_{k}_{b}")
        E.tensor_tensor(SC_[pi], SA[pi][:, 512:1024], v_pl[b], ALU.mult)
        ctx.inc(E.tensor_tensor_scan(Q_pl[b][:, 1:513], SA[pi][:, 0:512],
                                     SC_[pi], q_carry[:, b:b + 1],
                                     ALU.mult, ALU.add),
                "v", f"qscan_{k}_{b}")

    def emit_vector(E, ctx):
        ctx.wait(E, "g_init")
        ctx.wait(E, "c_consts")
        E.tensor_scalar(negu, cat[:, 32:40], -1.0, None, ALU.mult)
        E.tensor_tensor(uwb, wbcol.broadcast_to([P, 8]), cat[:, 32:40], ALU.add)
        for i, (dsthi, dstlo, src, cols) in enumerate(conv):
            if dstlo is None:
                continue
            pi = i % 2
            st = wslot[i % NSLOT].bitcast(F32)[:, 0:cols]
            ctx.wait(E, f"convhi_{i}")
            ctx.inc(E.tensor_tensor(conv_lo[pi][:, 0:cols], st,
                                    conv_hi[pi][:, 0:cols], ALU.subtract),
                    "v", f"conv_{i}")
        for k in range(nch):
            warm = (k == 0)
            E.tensor_copy(ln_all[:, 0:DB * 513:513], ln_carry)
            ctx.wait(E, f"sqm1_{k}")
            ctx.inc(E.tensor_tensor(var_s, banks[1], var_s, ALU.subtract),
                    "v", f"var1_{k}")
            ctx.wait(E, f"rstd1_{k}")
            for d in range(DB):
                E.tensor_tensor(ln_pl[d][:, 1:513], x_pl[d], banks[0],
                                ALU.subtract)
                ctx.inc(E.tensor_mul(ln_pl[d][:, 1:513], ln_pl[d][:, 1:513],
                                     rstd_s), "v",
                        f"ln1_{k}" if d == DB - 1 else None)
            for d in range(DB):
                ctx.inc(E.tensor_tensor(dd_pl[d], ln_pl[d][:, 1:513],
                                        ln_pl[d][:, 0:512], ALU.subtract),
                        "v", f"dd_{k}" if d == DB - 1 else None)
            projs = [("k", tmk), ("v", tmv)] if warm else \
                    [("k", tmk), ("v", tmv), ("r", tmr)]
            prev = None
            for pname, tm in projs:
                for d in range(DB):
                    if prev is not None:
                        ctx.wait(E, f"mixhi_{prev}_{k}_{d}")
                    ctx.inc(E.scalar_tensor_tensor(mixscr[d], dd_pl[d], tm[d],
                                                   ln_pl[d][:, 0:512],
                                                   ALU.mult, ALU.add),
                            "v", f"mixf_{pname}_{k}" if d == DB - 1 else None)
                for d in range(DB):
                    ctx.wait(E, f"mixhi_{pname}_{k}_{d}")
                    ctx.inc(E.tensor_tensor(mix_lo[d], mixscr[d], mix_hi[d],
                                            ALU.subtract), "v",
                            f"mixlo_{pname}_{k}" if d == DB - 1 else None)
                prev = pname
            E.tensor_copy(ln_carry, ln_all[:, 512:DB * 513:513])
            E.tensor_copy(P_all[:, 0:DB * 513:513], p_carry)
            E.tensor_copy(Q_all[:, 0:DB * 513:513], q_carry)
            for b in range(DB):
                pi = b % 2
                ctx.wait(E, f"uk_{k}_{b}")
                ctx.inc(E.tensor_tensor_scan(
                    P_pl[b][:, 1:513], wdec[b].broadcast_to([P, C]), uk_pl[b],
                    p_carry[:, b:b + 1], ALU.add, ALU.max),
                    "v", f"pscan_{k}_{b}")
                E.scalar_tensor_tensor(SA[pi][:, 0:512], P_pl[b][:, 0:512],
                                       wdec[b], P_pl[b][:, 1:513],
                                       ALU.add, ALU.subtract)
                ctx.wait(E, f"vq_{k}_{b}")
                ctx.inc(E.tensor_tensor(SA[pi][:, 512:1024], uk_pl[b],
                                        P_pl[b][:, 1:513], ALU.subtract),
                        "v", f"scanargs_{k}_{b}")
                if b >= 1:
                    _qscan(E, ctx, k, b - 1)
            _qscan(E, ctx, k, DB - 1)
            E.tensor_copy(p_carry, P_all[:, 512:DB * 513:513])
            ctx.inc(E.tensor_copy(q_carry, Q_all[:, 512:DB * 513:513]),
                    "v", f"carrysave_{k}")
            if warm:
                continue
            for b in range(DB):
                pi = b % 2
                if b >= 2:
                    ctx.wait(E, f"yhi_{k}_{b - 2}")
                ctx.wait(E, f"rq_{k}_{b}")
                E.scalar_tensor_tensor(SM[pi], P_pl[b][:, 0:512],
                                       negu[:, b:b + 1], uk_pl[b],
                                       ALU.add, ALU.max)
                E.scalar_tensor_tensor(SA[pi][:, 0:512], SM[pi], ucol[b],
                                       P_pl[b][:, 0:512], ALU.add, ALU.subtract)
                ctx.inc(E.tensor_tensor(SA[pi][:, 512:1024], SM[pi], uk_pl[b],
                                        ALU.subtract), "v", f"oargs_{k}_{b}")
                ctx.wait(E, f"oexp_{k}_{b}")
                ctx.inc(E.scalar_tensor_tensor(SC_[pi], SA[pi][:, 0:512],
                                               epsw[:, 0:1],
                                               SA[pi][:, 512:1024],
                                               ALU.add, ALU.add),
                        "v", f"oden_{k}_{b}")
                ctx.wait(E, f"oexpd_{k}_{b}")
                E.tensor_mul(SM[pi], SA[pi][:, 0:512], Q_pl[b][:, 0:512])
                E.tensor_tensor(SA[pi][:, 512:1024], SA[pi][:, 512:1024],
                                v_pl[b], ALU.mult)
                E.tensor_tensor(SM[pi], SM[pi], SA[pi][:, 512:1024], ALU.add)
                E.tensor_mul(SM[pi], SM[pi], SC_[pi])
                ctx.inc(E.tensor_mul(SM[pi], SM[pi], r_pl[b]), "v",
                        f"yf_{k}_{b}")
                ctx.wait(E, f"yhi_{k}_{b}")
                ctx.inc(E.tensor_tensor(mix_lo[b], SM[pi], mix_hi[b],
                                        ALU.subtract), "v", f"ylo_{k}_{b}")
            for e in range(DB):
                ctx.wait(E, f"wo_{k}_{e}")
                ctx.inc(E.tensor_tensor(x_pl[e], x_pl[e], banks[2 + (e % 4)],
                                        ALU.add), "v", f"woresid_{k}_{e}")
            ctx.wait(E, f"sqm2_{k}")
            ctx.inc(E.tensor_tensor(var_s, banks[1], var_s, ALU.subtract),
                    "v", f"var2_{k}")
            ctx.wait(E, f"rstd2_{k}")
            for d in range(DB):
                pi = d % 2
                E.tensor_tensor(SC_[pi], x_pl[d], banks[0], ALU.subtract)
                ctx.inc(E.tensor_mul(SM[pi], SC_[pi], rstd_s), "v",
                        f"ln2f_{k}_{d}")
                ctx.wait(E, f"ln2hi_{k}_{d}")
                ctx.inc(E.tensor_tensor(mix_lo[d], SM[pi], mix_hi[d],
                                        ALU.subtract), "v", f"ln2lo_{k}_{d}")
            for e in range(HB):
                pi = e % 2
                if e >= 2:
                    ctx.wait(E, f"h0hi_{k}_{e - 2}")
                ctx.wait(E, f"silu_{k}_{e}")
                ctx.wait(E, f"w2_{k}_{e}")
                ctx.inc(E.scalar_tensor_tensor(f_h[pi],
                                               banks[3 + 2 * (e % 2)], s2c[e],
                                               f_silu[pi], ALU.mult, ALU.mult),
                        "v", f"hf_{k}_{e}")
                ctx.wait(E, f"h0hi_{k}_{e}")
                ctx.inc(E.tensor_tensor(h0_lo[e], f_h[pi], h0_hi[e],
                                        ALU.subtract), "v", f"h0lo_{k}_{e}")
            for e in range(HB):
                pi = e % 2
                ctx.wait(E, f"g1hi_{k}_{e}")
                ctx.inc(E.tensor_tensor(g1_lo[e], f_g[pi], g1_hi[e],
                                        ALU.subtract), "v", f"g1lo_{k}_{e}")
            for e in range(HB):
                pi = e % 2
                ctx.wait(E, f"g2hi_{k}_{e}")
                ctx.inc(E.tensor_tensor(h0_lo[e], f_g[pi], h0_hi[e],
                                        ALU.subtract), "v", f"g2lo_{k}_{e}")
            for e in range(DB):
                ctx.wait(E, f"w3_{k}_{e}")
                ctx.inc(E.scalar_tensor_tensor(x_pl[e], banks[2 + (e % 4)],
                                               s3c[e], x_pl[e],
                                               ALU.mult, ALU.add),
                        "v", f"w3resid_{k}_{e}")

    def emit_scalar(E, ctx):
        ctx.wait(E, "g_init")
        ctx.wait(E, "c_consts")
        for i, (dsthi, dstlo, src, cols) in enumerate(conv):
            pi = i % 2
            st = wslot[i % NSLOT].bitcast(F32)[:, 0:cols]
            ctx.wait(E, f"convin_{i}")
            if i >= 2:
                ctx.wait(E, f"convw_{i - 2}")
            if dstlo is not None:
                ctx.inc(E.activation(conv_hi[pi][:, 0:cols], st, AF.Copy,
                                     bias=0.0, scale=1.0), "s", f"convhi_{i}")
            else:
                ctx.inc(E.activation(conv_hi[pi][:, 0:cols], st, AF.Copy,
                                     bias=0.0, scale=1.0), "s", f"conv_{i}")
        for k in range(nch):
            warm = (k == 0)
            ctx.wait(E, f"xin_{k}")
            if k == 0:
                ctx.wait(E, f"convw_{NCONV - 1}")
                ctx.wait(E, f"convw_{NCONV - 2}")
            elif k == 1:
                ctx.wait(E, "carrysave_0")
            else:
                ctx.wait(E, f"w3resid_{k - 1}_{DB - 1}")
            for d in range(DB):
                ctx.inc(E.activation(sq_pl[d], x_pl[d], AF.Square,
                                     bias=0.0, scale=1.0), "s",
                        f"sq1_{k}" if d == DB - 1 else None)
            ctx.wait(E, f"stats1q_{k}")
            ctx.inc(E.activation(var_s, banks[0], AF.Square, bias=0.0,
                                 scale=1.0), "s", f"sqm1_{k}")
            ctx.wait(E, f"var1_{k}")
            E.activation(rstd_s, var_s, AF.Ln, bias=epsln[:, 0:1], scale=1.0)
            E.drain()
            ctx.inc(E.activation(rstd_s, rstd_s, AF.Exp, bias=0.0, scale=-0.5),
                    "s", f"rstd1_{k}")
            def mixhis(pname):
                ctx.wait(E, f"mixf_{pname}_{k}")
                for d in range(DB):
                    ctx.inc(E.activation(mix_hi[d], mixscr[d], AF.Copy,
                                         bias=0.0, scale=1.0), "s",
                            f"mixhi_{pname}_{k}_{d}")

            mixhis("k")
            for e in range(DB):
                ctx.wait(E, f"kproj_{k}_{e}")
                bias = uwb[:, e:e + 1] if warm else ucol[e]
                ctx.inc(E.activation(uk_pl[e], banks[2 + (e % 4)], AF.Identity,
                                     bias=bias, scale=1.0), "s", f"uk_{k}_{e}")
            mixhis("v")
            for e in range(DB):
                ctx.wait(E, f"vproj_{k}_{e}")
                ctx.inc(E.activation(v_pl[e], banks[2 + (e % 4)], AF.Copy,
                                     bias=0.0, scale=1.0), "s", f"vq_{k}_{e}")
            if not warm:
                mixhis("r")
            for b in range(DB):
                pi = b % 2
                ctx.wait(E, f"scanargs_{k}_{b}")
                ctx.inc(E.activation(SA[pi], SA[pi], AF.Exp, bias=0.0,
                                     scale=1.0), "s", f"sexp_{k}_{b}")
            if warm:
                continue
            for e in range(DB):
                ctx.wait(E, f"rproj_{k}_{e}")
                ctx.inc(E.activation(r_pl[e], banks[2 + (e % 4)], AF.Sigmoid,
                                     bias=0.0, scale=1.0), "s", f"rq_{k}_{e}")
            for b in range(DB):
                pi = b % 2
                ctx.wait(E, f"oargs_{k}_{b}")
                ctx.inc(E.activation(SA[pi], SA[pi], AF.Exp, bias=0.0,
                                     scale=-1.0), "s", f"oexp_{k}_{b}")
                ctx.wait(E, f"oden_{k}_{b}")
                E.activation(SC_[pi], SC_[pi], AF.Ln, bias=0.0, scale=1.0)
                E.drain()
                ctx.inc(E.activation(SC_[pi], SC_[pi], AF.Exp, bias=0.0,
                                     scale=-1.0), "s", f"oexpd_{k}_{b}")
                ctx.wait(E, f"yf_{k}_{b}")
                ctx.inc(E.activation(mix_hi[b], SM[pi], AF.Copy, bias=0.0,
                                     scale=1.0), "s", f"yhi_{k}_{b}")
            for d in range(DB):
                ctx.wait(E, f"woresid_{k}_{d}")
                ctx.inc(E.activation(sq_pl[d], x_pl[d], AF.Square, bias=0.0,
                                     scale=1.0), "s",
                        f"sq2_{k}" if d == DB - 1 else None)
            ctx.wait(E, f"stats2q_{k}")
            ctx.inc(E.activation(var_s, banks[0], AF.Square, bias=0.0,
                                 scale=1.0), "s", f"sqm2_{k}")
            ctx.wait(E, f"var2_{k}")
            E.activation(rstd_s, var_s, AF.Ln, bias=epsln[:, 0:1], scale=1.0)
            E.drain()
            ctx.inc(E.activation(rstd_s, rstd_s, AF.Exp, bias=0.0, scale=-0.5),
                    "s", f"rstd2_{k}")
            for d in range(DB):
                pi = d % 2
                ctx.wait(E, f"ln2f_{k}_{d}")
                ctx.inc(E.activation(mix_hi[d], SM[pi], AF.Copy, bias=0.0,
                                     scale=1.0), "s", f"ln2hi_{k}_{d}")
            for e in range(HB):
                pi = e % 2
                ctx.wait(E, f"w1_{k}_{e}")
                if e >= 2:
                    ctx.wait(E, f"hf_{k}_{e - 2}")
                ctx.inc(E.activation(f_silu[pi], banks[2 + 2 * (e % 2)],
                                     AF.Silu, bias=0.0, scale=s1c[e]), "s",
                        f"silu_{k}_{e}")
                ctx.wait(E, f"hf_{k}_{e}")
                ctx.inc(E.activation(h0_hi[e], f_h[pi], AF.Copy, bias=0.0,
                                     scale=1.0), "s", f"h0hi_{k}_{e}")
            for e in range(HB):
                pi = e % 2
                ctx.wait(E, f"h1_{k}_{e}")
                if e >= 2:
                    ctx.wait(E, f"g1lo_{k}_{e - 2}")
                ctx.inc(E.activation(f_g[pi], banks[2 + (e % 4)], AF.Gelu,
                                     bias=0.0, scale=h1sc[e]), "s",
                        f"gelu1_{k}_{e}")
                ctx.inc(E.activation(g1_hi[e], banks[2 + (e % 4)], AF.Gelu,
                                     bias=0.0, scale=h1sc[e]), "s",
                        f"g1hi_{k}_{e}")
            for e in range(HB):
                pi = e % 2
                ctx.wait(E, f"h2_{k}_{e}")
                if e >= 2:
                    ctx.wait(E, f"g2lo_{k}_{e - 2}")
                ctx.inc(E.activation(f_g[pi], banks[2 + (e % 4)], AF.Gelu,
                                     bias=0.0, scale=h2sc[e]), "s",
                        f"gelu2_{k}_{e}")
                ctx.inc(E.activation(h0_hi[e], banks[2 + (e % 4)], AF.Gelu,
                                     bias=0.0, scale=h2sc[e]), "s",
                        f"g2hi_{k}_{e}")

    def emit_tensor(E, ctx):
        last_rd = {i: None for i in range(8)}
        wf = [0]

        def free_slot():
            ctx.alias(f"wfree_{wf[0]}", "t")
            wf[0] += 1

        def mm_group(bank_i, rhs_list, ev):
            ctx.wait_maybe(E, last_rd[bank_i])
            n = len(rhs_list)
            for i, (lhs, rhs) in enumerate(rhs_list):
                inst = E.matmul(banks[bank_i], lhs, rhs,
                                start=(i == 0), stop=(i == n - 1))
            ctx.inc(inst, "t", ev)

        def attn_rl(slot):
            rl = []
            for db in range(DB):
                wh = slot[:, db * P:(db + 1) * P]
                wl = slot[:, D + db * P:D + (db + 1) * P]
                rl.append((wh, mix_hi[db]))
                rl.append((wh, mix_lo[db]))
                rl.append((wl, mix_hi[db]))
            return rl

        ctx.wait(E, "g_init")
        for k in range(nch):
            warm = (k == 0)
            ctx.wait(E, f"xin_{k}")
            mm_group(0, [(ones, x_pl[d]) for d in range(DB)], f"stats1m_{k}")
            last_rd[0] = f"ln1_{k}"
            ctx.wait(E, f"sq1_{k}")
            mm_group(1, [(ones, sq_pl[d]) for d in range(DB)], f"stats1q_{k}")
            last_rd[1] = f"var1_{k}"
            projs = [("k", "kproj"), ("v", "vproj")] if warm else \
                    [("k", "kproj"), ("v", "vproj"), ("r", "rproj")]
            for pname, evp in projs:
                ctx.wait(E, f"mixlo_{pname}_{k}")
                for e in range(DB):
                    li = load_idx[("at", pname, e, k)]
                    ctx.wait(E, f"wl_{li}")
                    bi = 2 + (e % 4)
                    mm_group(bi, attn_rl(wslot[li % NSLOT]), f"{evp}_{k}_{e}")
                    last_rd[bi] = {"kproj": f"uk_{k}_{e}",
                                   "vproj": f"vq_{k}_{e}",
                                   "rproj": f"rq_{k}_{e}"}[evp]
                    free_slot()
            if warm:
                continue
            ctx.wait(E, f"ylo_{k}_{DB - 1}")
            for e in range(DB):
                li = load_idx[("at", "o", e, k)]
                ctx.wait(E, f"wl_{li}")
                bi = 2 + (e % 4)
                mm_group(bi, attn_rl(wslot[li % NSLOT]), f"wo_{k}_{e}")
                last_rd[bi] = f"woresid_{k}_{e}"
                free_slot()
            ctx.wait(E, f"woresid_{k}_{DB - 1}")
            mm_group(0, [(ones, x_pl[d]) for d in range(DB)], f"stats2m_{k}")
            last_rd[0] = f"ln2f_{k}_{DB - 1}"
            ctx.wait(E, f"sq2_{k}")
            mm_group(1, [(ones, sq_pl[d]) for d in range(DB)], f"stats2q_{k}")
            last_rd[1] = f"var2_{k}"
            ctx.wait(E, f"ln2lo_{k}_{DB - 1}")
            for e in range(HB):
                li = load_idx[("w12", e, k)]
                ctx.wait(E, f"wl_{li}")
                slot = wslot[li % NSLOT]
                rl1, rl2 = [], []
                for db in range(DB):
                    wa = slot[:, db * P:(db + 1) * P]
                    wb = slot[:, D + db * P:D + (db + 1) * P]
                    rl1.append((wa, mix_hi[db]))
                    rl1.append((wa, mix_lo[db]))
                    rl2.append((wb, mix_hi[db]))
                    rl2.append((wb, mix_lo[db]))
                b1i = 2 + 2 * (e % 2)
                b2i = 3 + 2 * (e % 2)
                mm_group(b1i, rl1, f"w1_{k}_{e}")
                last_rd[b1i] = f"silu_{k}_{e}"
                mm_group(b2i, rl2, f"w2_{k}_{e}")
                last_rd[b2i] = f"hf_{k}_{e}"
                free_slot()
            ctx.wait(E, f"h0lo_{k}_{HB - 1}")
            ctx.wait(E, f"h0hi_{k}_{HB - 1}")
            for e in range(HB):
                li = load_idx[("h1", e, k)]
                ctx.wait(E, f"wl_{li}")
                slot = wslot[li % NSLOT]
                rl = []
                for hb in range(HB):
                    w = slot[:, hb * P:(hb + 1) * P]
                    rl.append((w, h0_hi[hb]))
                    rl.append((w, h0_lo[hb]))
                bi = 2 + (e % 4)
                mm_group(bi, rl, f"h1_{k}_{e}")
                last_rd[bi] = f"g1hi_{k}_{e}"
                free_slot()
            ctx.wait(E, f"g1lo_{k}_{HB - 1}")
            ctx.wait(E, f"g1hi_{k}_{HB - 1}")
            for e in range(HB):
                li = load_idx[("h2", e, k)]
                ctx.wait(E, f"wl_{li}")
                slot = wslot[li % NSLOT]
                rl = []
                for hb in range(HB):
                    w = slot[:, hb * P:(hb + 1) * P]
                    rl.append((w, g1_hi[hb]))
                    rl.append((w, g1_lo[hb]))
                bi = 2 + (e % 4)
                mm_group(bi, rl, f"h2_{k}_{e}")
                last_rd[bi] = f"g2hi_{k}_{e}"
                free_slot()
            ctx.wait(E, f"g2lo_{k}_{HB - 1}")
            ctx.wait(E, f"g2hi_{k}_{HB - 1}")
            for e in range(DB):
                li = load_idx[("w3", e, k)]
                ctx.wait(E, f"wl_{li}")
                slot = wslot[li % NSLOT]
                rl = []
                for hb in range(HB):
                    w = slot[:, hb * P:(hb + 1) * P]
                    rl.append((w, h0_hi[hb]))
                    rl.append((w, h0_lo[hb]))
                bi = 2 + (e % 4)
                mm_group(bi, rl, f"w3_{k}_{e}")
                last_rd[bi] = f"w3resid_{k}_{e}"
                free_slot()

    ctx1 = Ctx(plan, emit=False)
    emit_sync(Mock(), ctx1)
    emit_gpsimd(Mock(), ctx1)
    emit_vector(Mock(), ctx1)
    emit_scalar(Mock(), ctx1)
    emit_tensor(Mock(), ctx1)

    ctx2 = Ctx(plan, emit=True, handles=handles)
    with nc.Block() as block:
        @block.sync
        def _(eng):
            emit_sync(eng, ctx2)

        @block.gpsimd
        def _(eng):
            emit_gpsimd(Guard(eng), ctx2)

        @block.vector
        def _(eng):
            emit_vector(Guard(eng), ctx2)

        @block.scalar
        def _(eng):
            emit_scalar(Guard(eng), ctx2)

        @block.tensor
        def _(eng):
            emit_tensor(eng, ctx2)

    for s, v in ctx1.cnt.items():
        assert ctx2.cnt[s] == v, f"pass mismatch {s}: {v} vs {ctx2.cnt[s]}"
    for g in guards:
        g.__exit__(None, None, None)
    return nc


_NC_CACHE = [None]


def _tile_proj(w, eb, kb):
    t = np.asarray(w, np.float32).reshape(eb, 128, kb, 128).transpose(0, 3, 2, 1)
    return np.ascontiguousarray(t.reshape(eb, 128, kb * 128))


def kernel(L0_attn_x, L0_attn_time_mix_k, L0_attn_time_mix_v,
           L0_attn_time_mix_r, L0_attn_W_k, L0_attn_W_v, L0_attn_W_r,
           L0_attn_W_o, L0_attn_time_decay, L0_attn_time_first,
           L0_ffn_w1_w, L0_ffn_w1_s, L0_ffn_w2_w, L0_ffn_w2_s,
           L0_ffn_h1_w, L0_ffn_h1_s, L0_ffn_h2_w, L0_ffn_h2_s,
           L0_ffn_w3_w, L0_ffn_w3_s):
    x = np.asarray(L0_attn_x, dtype=np.float32)
    B, T, Dm = x.shape
    assert (B, T, Dm) == (2, 16384, 1024)

    wkt = _tile_proj(L0_attn_W_k, DB, DB)
    wvt = _tile_proj(L0_attn_W_v, DB, DB)
    wrt = _tile_proj(L0_attn_W_r, DB, DB)
    wot = _tile_proj(L0_attn_W_o, DB, DB)
    w1t = _tile_proj(L0_ffn_w1_w, HB, DB)
    w2t = _tile_proj(L0_ffn_w2_w, HB, DB)
    h1t = _tile_proj(L0_ffn_h1_w, HB, HB)
    h2t = _tile_proj(L0_ffn_h2_w, HB, HB)
    w3t = _tile_proj(L0_ffn_w3_w, DB, HB)

    def cp(v, nb):
        return np.ascontiguousarray(np.asarray(v, np.float32).reshape(nb, 128).T)

    cattn = np.concatenate([cp(L0_attn_time_mix_k, 8), cp(L0_attn_time_mix_v, 8),
                            cp(L0_attn_time_mix_r, 8), cp(L0_attn_time_decay, 8),
                            cp(L0_attn_time_first, 8)], axis=1)
    cffn = np.concatenate([cp(L0_ffn_w1_s, 32), cp(L0_ffn_w2_s, 32),
                           cp(L0_ffn_h1_s, 32), cp(L0_ffn_h2_s, 32),
                           cp(L0_ffn_w3_s, 8)], axis=1)

    in_maps = []
    for c in range(8):
        b, q = c // 4, c % 4
        t0 = q * TOK
        xTc = np.zeros((D, NCH * C), np.float32)
        if t0 - C < 0:
            xTc[:, C:] = x[b, 0:t0 + TOK, :].T
        else:
            xTc[:, :] = x[b, t0 - C:t0 + TOK, :].T
        cwb = np.full((P, 1), -1.0e4 if q == 0 else 0.0, np.float32)
        in_maps.append({
            "xT": np.ascontiguousarray(xTc),
            "wkt": wkt, "wvt": wvt, "wrt": wrt, "wot": wot,
            "w1t": w1t, "w2t": w2t, "h1t": h1t, "h2t": h2t, "w3t": w3t,
            "cattn": cattn, "cffn": cffn, "cwb": cwb,
        })

    if _NC_CACHE[0] is None:
        _NC_CACHE[0] = build_program()
    nc = _NC_CACHE[0]
    res = run_bass_kernel_spmd(nc, in_maps, core_ids=list(range(8)))

    out = np.empty((B, T, Dm), np.float32)
    for c in range(8):
        b, q = c // 4, c % 4
        t0 = q * TOK
        out[b, t0:t0 + TOK, :] = np.asarray(res.results[c]["outT"]).T
    return out
